# version 27
# baseline (speedup 1.0000x reference)
import numpy as np
import ml_dtypes
import concourse.bass as bass
import concourse.mybir as mybir
from concourse.bass_utils import run_bass_kernel_spmd
from contextlib import ExitStack

_NCORES = 8

F32 = mybir.dt.float32
BF16 = mybir.dt.bfloat16
AX = mybir.AxisListType.X
AF = mybir.ActivationFunctionType
ALU = mybir.AluOpType

T = 2048
D = 1024
NT = 16
KT = 8
NCH = 4
NE = 8
I_DIM = 938
NMT = 16
NIT = 8

ENGS = ("sync", "gpsimd", "tensor", "vector", "scalar")


class Prog:
    def __init__(self, nc):
        self.nc = nc
        self.ops = []
        self.sems = {}
        self.cnt = {}
        self.ctx = ExitStack()

    def sem(self, name):
        if name not in self.sems:
            self.sems[name] = self.ctx.enter_context(self.nc.semaphore(name))
            self.cnt[name] = 0
        return self.sems[name]

    def op(self, eng, emit, waits=(), inc=None, inc_n=1):
        wl = [(self.sem(s), v) for (s, v) in waits if v > 0]
        post = None
        if inc is not None:
            self.sem(inc)
            self.cnt[inc] += inc_n
            post = self.cnt[inc]
        self.ops.append((eng, emit, wl, (self.sems[inc], inc_n) if inc else None))
        return post

    def c(self, name):
        self.sem(name)
        return self.cnt[name]

    def realize(self):
        nc = self.nc
        with nc.Block() as blk:
            for eng_name in ENGS:
                eng_ops = [o for o in self.ops if o[0] == eng_name]
                if not eng_ops:
                    continue
                def make(eng_ops):
                    def section(eng):
                        last_wait = {}
                        for (_, emit, wl, inc) in eng_ops:
                            for (sem, val) in wl:
                                if last_wait.get(id(sem), -1) >= val:
                                    continue
                                eng.wait_ge(sem, val)
                                last_wait[id(sem)] = val
                            inst = emit(eng)
                            if inc is not None:
                                assert inst is not None
                                inst.then_inc(inc[0], inc[1])
                    return section
                getattr(blk, eng_name)(make(eng_ops))


def host_prep(inp):
    f32 = np.float32
    d = {}
    gup_list = [np.asarray(inp["L0_ffn_shared_gate_up"], f32)] + \
               [np.asarray(inp["L0_ffn_experts_gate_up"][e], f32) for e in range(NE)]
    wd_list = [np.asarray(inp["L0_ffn_shared_down"], f32)] + \
              [np.asarray(inp["L0_ffn_experts_down"][e], f32) for e in range(NE)]
    gupT = np.zeros((9, D, 2048), f32)
    for j, w in enumerate(gup_list):
        g = w[:I_DIM]
        u = w[I_DIM:]
        gupT[j, :, 0:I_DIM] = g.T
        gupT[j, :, 1024:1024 + I_DIM] = u.T
    d["gup"] = gupT.reshape(9 * D, 2048)
    wdT = np.zeros((9, D, D), f32)
    for j, w in enumerate(wd_list):
        wdT[j, :I_DIM, :] = w.T
    d["wd"] = wdT.reshape(9 * D, D)
    d["gw"] = np.ascontiguousarray(np.asarray(inp["L0_ffn_gate_weight"], f32).T)
    for nm, key in (("wq", "L0_attn_Wq"), ("wk", "L0_attn_Wk"), ("wv", "L0_attn_Wv")):
        d[nm] = np.ascontiguousarray(np.asarray(inp[key], f32).T)
    woT = np.asarray(inp["L0_attn_Wo"], f32).T
    d["wo"] = np.ascontiguousarray(
        woT.reshape(4, 2, 128, D).transpose(0, 2, 1, 3).reshape(4 * 128, 2 * D))
    d["bqf"] = np.ascontiguousarray(np.asarray(inp["L0_attn_bq"], f32).reshape(8, 128).T)
    d["bkf"] = np.ascontiguousarray(np.asarray(inp["L0_attn_bk"], f32).reshape(8, 128).T)
    d["bvr"] = np.ascontiguousarray(np.broadcast_to(np.asarray(inp["L0_attn_bv"], f32), (128, D)))
    d["bor"] = np.ascontiguousarray(np.broadcast_to(np.asarray(inp["L0_attn_bo"], f32), (128, D)))
    d["ident"] = np.eye(128, dtype=ml_dtypes.bfloat16)
    mw = (np.arange(128)[:, None] <= (np.arange(896)[None, :] - 384)).astype(ml_dtypes.bfloat16)
    d["maskw"] = mw
    sel2 = np.zeros((65, 128), ml_dtypes.bfloat16)
    sel2[0, 0:64] = 1
    sel2[64, 64:128] = 1
    d["sel2"] = sel2
    return d


def build(stage=4, debug=False):
    nc = bass.Bass("TRN2", target_bir_lowering=False)
    dr = lambda n, s, dt=F32, kind="ExternalInput": nc.dram_tensor(n, s, dt, kind=kind).ap()
    x_h = dr("x", [T, D])
    gup_h = dr("gup", [9 * D, 2048])
    wd_h = dr("wd", [9 * D, D])
    gw_h = dr("gw", [D, 8])
    wq_h = dr("wq", [D, D]); wk_h = dr("wk", [D, D]); wv_h = dr("wv", [D, D])
    wo_h = dr("wo", [512, 2 * D])
    bqf_h = dr("bqf", [128, 8]); bkf_h = dr("bkf", [128, 8])
    bvr_h = dr("bvr", [128, D]); bor_h = dr("bor", [128, D])
    id_h = dr("ident", [128, 128], BF16)
    mw_h = dr("maskw", [128, 896], BF16)
    sel2_h = dr("sel2", [65, 128], BF16)
    out_h = dr("out", [T, D], F32, kind="ExternalOutput")
    if debug:
        qt_d = dr("qt_d", [128, 4096], BF16, kind="ExternalOutput")
        kt_d = dr("kt_d", [128, 4096], BF16, kind="ExternalOutput")
        vs_d = dr("vs_d", [128, 4096], BF16, kind="ExternalOutput")
        at_d = dr("at_d", [128, 8192], BF16, kind="ExternalOutput")

    P = Prog(nc)
    ctx = P.ctx
    sb = lambda n, s, dt=F32: ctx.enter_context(nc.sbuf_tensor(n, s, dt))
    psum = lambda n, s, dt=F32: ctx.enter_context(nc.psum_tensor(n, s, dt))

    x_res = sb("x_res", [128, NT * D])
    zT = sb("zT", [128, KT * T], BF16)
    z_tm = sb("z_tm", [128, 2 * D], BF16)
    sq_scr = sb("sq_scr", [128, D], BF16)
    stat = sb("stat", [128, 144])
    logit_junk = None
    rE = sb("rE", [128, NT * 8])
    rsc = sb("rsc", [128, 96])
    logit_sb = sb("logit_sb", [128, NT * 8])
    msk8 = sb("msk8", [128, 8])
    cw = sb("cw", [128, NT * 8])
    consts = sb("consts", [128, 8])
    bqf = sb("bqf_sb", [128, 8]); bkf = sb("bkf_sb", [128, 8])
    bvr = sb("bvr_sb", [128, D]); bor = sb("bor_sb", [128, D])
    ident = sb("ident_sb", [128, 128], BF16)
    maskw = sb("maskw_sb", [128, 896], BF16)
    gwT = sb("gwT_sb", [128, KT * 8], BF16)
    sel2 = sb("sel2_sb", [128, 128], BF16)

    PS_A = psum("PS_A", [128, 1024])
    PS_B = psum("PS_B", [128, 1024])
    D0 = psum("D0", [128, 1024])
    D1 = psum("D1", [128, 1024])
    Gs = [PS_A[:, 0:512], PS_A[:, 512:1024]]
    G_BID = ["A0", "A1"]
    _bcB = PS_B[:].bitcast(BF16)
    Ts = [_bcB[:, 0:1024], _bcB[:, 1024:2048]]
    T_BID = ["B0", "B1"]
    SS = [PS_A, PS_B]
    SS_BID = [("A0", "A1"), ("B0", "B1")]

    bank_hist = {}

    def use_banks(waits, *bank_ids):
        for b in bank_ids:
            if b in bank_hist:
                waits.append(bank_hist[b])

    x3 = x_h.rearrange("(i p) c -> p i c", p=128)
    xr3 = x_res[:].rearrange("p (i c) -> p i c", c=D)
    for q, eng in enumerate(("sync", "scalar", "sync", "scalar")):
        P.op(eng, lambda e, q=q: e.dma_start(
            out=xr3[:, q * 4:(q + 1) * 4, :], in_=x3[:, q * 4:(q + 1) * 4, :]),
             inc="x%d" % q, inc_n=16)
    for dst, src in ((bqf, bqf_h), (bkf, bkf_h), (bvr, bvr_h), (bor, bor_h),
                     (ident, id_h), (maskw, mw_h)):
        P.op("sync", lambda e, d=dst, s=src: e.dma_start(out=d[:], in_=s), inc="ld", inc_n=16)
    LD_ALL = P.c("ld")
    P.op("gpsimd", lambda e: e.dma_start(
        out=gwT[:].rearrange("p (kt c) -> p kt c", kt=KT),
        in_=gw_h.rearrange("(kt p) c -> p kt c", p=128)), inc="wgw", inc_n=16)
    P.op("vector", lambda e: e.memset(consts[:, 0:1], 0.5))
    P.op("vector", lambda e: e.memset(consts[:, 1:2], -8.0))
    P.op("vector", lambda e: e.memset(consts[:, 2:3], 1e-8), inc="const")
    P.op("sync", lambda e: e.dma_start(out=sel2[0:65, :], in_=sel2_h), inc="ld", inc_n=16)
    CONST_OK = P.c("const")

    def ln_stats(i, waits):
        xall = [("x0", 16), ("x1", 16), ("x2", 16), ("x3", 16)]
        xs = x_res[:, i * D:(i + 1) * D]
        P.op("vector", lambda e, xs=xs, i=i: e.reduce_sum(stat[:, i:i+1], xs, axis=AX),
             waits=list(waits) + xall, inc="vsum")
        P.op("scalar", lambda e, xs=xs, i=i: e.activation(
            sq_scr[:], xs, AF.Square, accum_out=stat[:, 16+i:17+i]),
             waits=list(waits) + xall, inc="asq")

    def ln_chain(hb):
        hs = slice(hb * 8, hb * 8 + 8)
        sm, ss, m, m2, var = (stat[:, 0:16][:, hs], stat[:, 16:32][:, hs],
                              stat[:, 32:48][:, hs], stat[:, 48:64][:, hs],
                              stat[:, 64:80][:, hs])
        r_h, mr_h = stat[:, 96:112][:, hs], stat[:, 112:128][:, hs]
        P.op("vector", lambda e, m=m, sm=sm: e.tensor_scalar(m, sm, 1.0 / D, None, ALU.mult),
             waits=[("vsum", P.c("vsum")), ("asq", P.c("asq"))], inc="vx")
        P.op("vector", lambda e, m2=m2, m=m: e.tensor_tensor(m2, m, m, ALU.mult),
             waits=[("vx", P.c("vx"))], inc="vx")
        P.op("vector", lambda e, var=var, ss=ss, m2=m2: e.scalar_tensor_tensor(
            var, ss, 1.0 / D, m2, ALU.mult, ALU.subtract),
             waits=[("vx", P.c("vx"))], inc="vx")
        P.op("vector", lambda e, var=var: e.tensor_scalar(var, var, 1e-5, None, ALU.add),
             waits=[("vx", P.c("vx"))], inc="vx")
        P.op("vector", lambda e, var=var: e.reciprocal(var, var),
             waits=[("vx", P.c("vx"))], inc="vrv")
        P.op("scalar", lambda e, r_h=r_h, var=var: e.activation(r_h, var, AF.Sqrt),
             waits=[("vrv", P.c("vrv"))], inc="ar")
        P.op("vector", lambda e, mr_h=mr_h, m=m, r_h=r_h: e.scalar_tensor_tensor(
            mr_h, m, -1.0, r_h, ALU.mult, ALU.mult),
             waits=[("ar", P.c("ar"))], inc="vmr")
        return P.c("vmr")

    def ln_phase(xgate, zgate=(), pre=None):
        if pre is None:
            xg = xgate if callable(xgate) else (lambda i: list(xgate))
            vmr_half = []
            for hb in range(2):
                for i in range(hb * 8, hb * 8 + 8):
                    ln_stats(i, xg(i))
                vmr_half.append(ln_chain(hb))
        else:
            vmr_half = pre
        pzt_by = {}
        for i in range(NT):
            xs = x_res[:, i * D:(i + 1) * D]
            zbuf = z_tm[:, (i % 2) * D:(i % 2 + 1) * D]
            ztw = [("vmr", vmr_half[i // 8])]
            if i >= 2:
                ztw.append(("pzt", pzt_by[i - 2]))
            P.op("scalar", lambda e, xs=xs, zbuf=zbuf, i=i: e.activation(
                zbuf, xs, AF.Identity, bias=stat[:, 112+i:113+i], scale=stat[:, 96+i:97+i]),
                 waits=ztw, inc="az")
            tb = Ts[i % 2]
            bid = T_BID[i % 2]
            w = [("az", P.c("az")), ("ld", 80)]
            use_banks(w, bid)
            def tr_ops(e, zbuf=zbuf, tb=tb):
                last = None
                for kt in range(KT):
                    last = e.transpose(tb[:, kt * 128:(kt + 1) * 128],
                                       zbuf[:, kt * 128:(kt + 1) * 128], ident[:])
                return last
            P.op("tensor", tr_ops, waits=w, inc="pzt")
            pzt_by[i] = P.c("pzt")
            zdst = zT[:].rearrange("p (kt t) -> p kt t", kt=KT)[:, :, i * 128:(i + 1) * 128]
            zsrc = tb[:].rearrange("p (kt t) -> p kt t", kt=KT)
            zw = [("pzt", P.c("pzt"))] + (list(zgate) if i == 0 else [])
            pc = P.op("vector", lambda e, zdst=zdst, zsrc=zsrc: e.tensor_copy(zdst, zsrc),
                      waits=zw, inc="zev")
            bank_hist[bid] = ("zev", pc)
        return P.c("zev")

    def router_phase(zev_base):
        for i in range(NT):
            R = Gs[i % 2][:, 0:8]
            bid = G_BID[i % 2]
            w = [("zev", zev_base - NT + i + 1), ("wgw", 16)]
            use_banks(w, bid)
            def rmm(e, i=i, R=R):
                last = None
                for kt in range(KT):
                    last = e.matmul(R, zT[:, kt * T + i * 128: kt * T + (i + 1) * 128],
                                    gwT[:, kt * 8:(kt + 1) * 8],
                                    start=(kt == 0), stop=(kt == KT - 1))
                return last
            P.op("tensor", rmm, waits=w, inc="rmm")
            pc = P.op("vector", lambda e, R=R, i=i: e.tensor_copy(
                logit_sb[:, i * 8:(i + 1) * 8], R),
                 waits=[("rmm", P.c("rmm"))], inc="rlc")
            bank_hist[bid] = ("rlc", pc)
        l3 = logit_sb[:].rearrange("p (a b) -> p a b", b=8)
        rE3 = rE[:].rearrange("p (a b) -> p a b", b=8)
        cw3 = cw[:].rearrange("p (a b) -> p a b", b=8)
        Zs, m1, m2, den, rcp = (rsc[:, 0:16], rsc[:, 16:32], rsc[:, 32:48],
                                rsc[:, 48:64], rsc[:, 64:80])
        nmx = rsc[:, 80:96]
        P.op("vector", lambda e: e.reduce_max(
            nmx.rearrange("p (a b) -> p a b", b=1), l3, axis=AX, negate=True),
             waits=[("rlc", P.c("rlc"))], inc="vnm")
        VNM = P.c("vnm")
        for i in range(NT):
            P.op("scalar", lambda e, i=i: e.activation(
                rE[:, i * 8:(i + 1) * 8], logit_sb[:, i * 8:(i + 1) * 8], AF.Exp,
                bias=rsc[:, 80 + i:81 + i], scale=1.0, accum_out=rsc[:, i:i+1]),
                 waits=[("vnm", VNM), ("const", CONST_OK)], inc="rexp")
        REXP = P.c("rexp")
        m1_b = m1.unsqueeze(2).broadcast_to((128, NT, 8))
        m2_b = m2.unsqueeze(2).broadcast_to((128, NT, 8))
        rcp_b = rcp.unsqueeze(2).broadcast_to((128, NT, 8))
        P.op("vector", lambda e: e.reduce_max(
            m1.rearrange("p (a b) -> p a b", b=1), rE3, axis=AX),
             waits=[("rexp", REXP)], inc="vx")
        P.op("vector", lambda e: e.tensor_tensor(l3, rE3, m1_b, ALU.is_ge),
             waits=[("vx", P.c("vx"))], inc="vx")
        P.op("vector", lambda e: e.tensor_scalar(
            logit_sb[:], logit_sb[:], 1e30, None, ALU.mult))
        P.op("vector", lambda e: e.tensor_tensor(
            logit_sb[:], rE[:], logit_sb[:], ALU.subtract), inc="vx")
        P.op("vector", lambda e: e.reduce_max(
            m2.rearrange("p (a b) -> p a b", b=1), l3, axis=AX),
             waits=[("vx", P.c("vx"))], inc="vx")
        P.op("vector", lambda e: e.scalar_tensor_tensor(den, Zs, 1e-8, m1,
                                                        ALU.mult, ALU.add),
             waits=[("vx", P.c("vx"))], inc="vx")
        P.op("vector", lambda e: e.tensor_tensor(den, den, m2, ALU.add),
             waits=[("vx", P.c("vx"))], inc="vx")
        P.op("vector", lambda e: e.reciprocal(rcp, den),
             waits=[("vx", P.c("vx"))], inc="vx")
        P.op("vector", lambda e: e.tensor_scalar(rcp, rcp, 0.5, None, ALU.mult),
             waits=[("vx", P.c("vx"))], inc="vx")
        P.op("vector", lambda e: e.tensor_tensor(l3, rE3, m2_b, ALU.is_ge),
             waits=[("vx", P.c("vx"))], inc="vx")
        P.op("vector", lambda e: e.tensor_tensor(
            logit_sb[:], logit_sb[:], rE[:], ALU.mult))
        P.op("vector", lambda e: e.tensor_tensor(
            cw3, l3, rcp_b, ALU.mult),
             waits=[("vx", P.c("vx"))], inc="cw")
        return P.c("cw")

    arena = sb("arena", [128, 46336], BF16)
    def alloc_ffn():
        s = {}
        s["gupA"] = arena[:, 0:16384]
        s["gupB"] = arena[:, 16384:32768]
        s["wdT"] = arena[:, 32768:40960]
        s["sg"] = arena[:, 40960:45056]
        return s

    def dma_gup(s, j, waits):
        src = gup_h[j * D:(j + 1) * D, :].rearrange("(kt p) f -> p kt f", p=128)
        dst = s["gupA" if j % 2 == 0 else "gupB"][:].rearrange("p (kt f) -> p kt f", kt=KT)
        return P.op("gpsimd", lambda e: e.dma_start(out=dst, in_=src),
                    waits=waits, inc="wg", inc_n=16)

    def dma_wd(s, j, waits):
        src = wd_h[j * D:(j + 1) * D, :].rearrange("(kt p) f -> p kt f", p=128)
        dst = s["wdT"][:].rearrange("p (kt f) -> p kt f", kt=KT)
        return P.op("gpsimd", lambda e: e.dma_start(out=dst, in_=src),
                    waits=waits, inc="wd", inc_n=16)

    yev_by_tile = {}

    def expert(s, j, zev_base, cw_done, wg_need, wd_need, post_yev=None):
        gup = s["gupA" if j % 2 == 0 else "gupB"]
        MT_ORDER = [0, 8, 1, 9, 2, 10, 3, 11, 4, 12, 5, 13, 6, 14, 7, 15]
        for ch in range(NCH):
            for pos, mt in enumerate(MT_ORDER):
                gb = Gs[pos % 2]
                bid = G_BID[pos % 2]
                w = [("wg", wg_need), ("zev", zev_base - NT + (ch + 1) * 4)]
                use_banks(w, bid)
                def gmm(e, ch=ch, mt=mt, gb=gb, gup=gup):
                    last = None
                    for kt in range(KT):
                        last = e.matmul(
                            gb[:], gup[:, kt * 2048 + mt * 128: kt * 2048 + (mt + 1) * 128],
                            zT[:, kt * T + ch * 512: kt * T + (ch + 1) * 512],
                            start=(kt == 0), stop=(kt == KT - 1))
                    return last
                P.op("tensor", gmm, waits=w, inc="gmm")
                if mt < 8:
                    pc = P.op("scalar", lambda e, gb=gb, mt=mt: e.activation(
                        s["sg"][:, mt * 512:(mt + 1) * 512], gb[:], AF.Silu),
                         waits=[("gmm", P.c("gmm"))], inc="sgv")
                    bank_hist[bid] = ("sgv", pc)
                else:
                    it = mt - 8
                    pc = P.op("vector", lambda e, gb=gb, it=it: e.tensor_tensor(
                        s["sg"][:, it * 512:(it + 1) * 512],
                        s["sg"][:, it * 512:(it + 1) * 512],
                        gb[:], ALU.mult),
                         waits=[("gmm", P.c("gmm")), ("sgv", P.c("sgv"))], inc="amul")
                    bank_hist[bid] = ("amul", pc)
            amul_base = P.c("amul") - 8
            for tl in range(4):
                tt = ch * 4 + tl
                db = [D0, D1][tt % 2]
                bid0, bid1 = ("D%da" % (tt % 2)), ("D%db" % (tt % 2))
                def dmm(e, tl=tl, db=db, k0=0, k1=KT):
                    last = None
                    for kt in range(k0, k1):
                        for nh in range(2):
                            last = e.matmul(
                                db[:, nh * 512:(nh + 1) * 512],
                                s["sg"][:, kt * 512 + tl * 128: kt * 512 + (tl + 1) * 128],
                                s["wdT"][:, kt * 1024 + nh * 512: kt * 1024 + (nh + 1) * 512],
                                start=(kt == 0), stop=(kt == KT - 1))
                    return last
                for (k0, k1) in ((0, 4), (4, 7), (7, 8)):
                    w = [("amul", amul_base + k1), ("wd", wd_need)]
                    if k0 == 0:
                        use_banks(w, bid0, bid1)
                    P.op("tensor", lambda e, tl=tl, db=db, k0=k0, k1=k1: dmm(
                        e, tl, db, k0, k1), waits=w, inc="dmm")
                scal = consts[:, 0:1] if j == 0 else cw[:, tt * 8 + (j - 1): tt * 8 + j]
                wy = [("dmm", P.c("dmm"))]
                if j > 0:
                    wy.append(("cw", cw_done))
                pc = P.op("vector", lambda e, db=db, tt=tt, scal=scal: e.scalar_tensor_tensor(
                    x_res[:, tt * D:(tt + 1) * D], db[:], scal,
                    x_res[:, tt * D:(tt + 1) * D], ALU.mult, ALU.add),
                     waits=wy, inc="yev")
                yev_by_tile[tt] = pc
                bank_hist[bid0] = ("yev", pc)
                bank_hist[bid1] = ("yev", pc)
                if post_yev is not None:
                    post_yev(tt)

    def ffn_app(xgate, wgate, nexperts, zgate=(), pre=None, post_yev=None):
        zev = ln_phase(xgate, zgate, pre=pre)
        cw_done = router_phase(zev)
        s = alloc_ffn()
        dma_gup(s, 0, waits=list(wgate))
        dma_wd(s, 0, waits=list(wgate))
        if nexperts > 1:
            dma_gup(s, 1, waits=list(wgate))
        wg_counts = {0: P.c("wg") - (16 if nexperts > 1 else 0),
                     1: P.c("wg")}
        wd_count = P.c("wd")
        gup_done_hist = {}
        for j in range(nexperts):
            expert(s, j, zev, cw_done, wg_counts[min(j, 1)] if j <= 1 else wg_counts[j],
                   wd_count, post_yev=post_yev if j == nexperts - 1 else None)
            gup_done_hist[j] = P.c("gmm")
            down_done = P.c("dmm")
            if j + 2 < nexperts:
                dma_gup(s, j + 2, waits=[("gmm", gup_done_hist[j])])
                wg_counts[j + 2] = P.c("wg")
            if j + 1 < nexperts:
                dma_wd(s, j + 1, waits=[("dmm", down_done)])
                wd_count = P.c("wd")
        return P.c("yev")

    AQ, AK, AV, AE, AA, AWQ, AWK, AWV, AWO, ARD, AON, ADR = (
        0, 4096, 8192, 12352, 15424, 23616, 27712, 31808, 35904, 44096, 45696, 45760)

    def attention(xgate, wgate, zgate, pre=None, post_pjev=None):
        zev2 = ln_phase(xgate, zgate, pre=pre)
        qT = arena[:, AQ:AQ + 4096]
        kT = arena[:, AK:AK + 4096]
        vS = arena[:, AV:AV + 16 * 256]
        aT = arena[:, AA:AA + 8192]
        rdB = arena[:, ARD:ARD + 1536]
        onesA = arena[:, AON:AON + 64]
        drB = arena[:, ADR:ADR + 512]
        woBs = [arena[:, AWO:AWO + 2048], arena[:, AWO + 2048:AWO + 4096]]
        wBs = [{"q": arena[:, AWQ:AWQ + 2048], "k": arena[:, AWK:AWK + 2048],
                "v": arena[:, AWV:AWV + 2048]},
               {"q": arena[:, AWQ + 2048:AWQ + 4096], "k": arena[:, AWK + 2048:AWK + 4096],
                "v": arena[:, AWV + 2048:AWV + 4096]}]
        O2s = [D0[:, 0:512], D1[:, 512:1024]]
        O2_BID = ["D0a", "D1b"]
        dps = D0[:, 512:1024]
        DR2 = D1[:, 0:512]
        P.op("vector", lambda e: e.memset(onesA[:], 1.0), waits=list(wgate), inc="vset")
        P.op("vector", lambda e: e.memset(rdB[0:64, :], 0.0), waits=list(wgate))
        qmm_after = {}
        pjmm_after = {}
        wqkv_after = {}

        def dma_attn_w(g):
            wg_waits = list(wgate) if g < 2 else \
                [("qmm", qmm_after[g - 2]), ("pjmm", pjmm_after[g - 2])]
            for nm, wh in (("q", wq_h), ("k", wk_h), ("v", wv_h)):
                src = wh.rearrange("(kt p) e -> p kt e", p=128)[:, :, 256 * g:256 * (g + 1)]
                dst = wBs[g % 2][nm][:].rearrange("p (kt e) -> p kt e", kt=KT)
                P.op("gpsimd", lambda e, src=src, dst=dst: e.dma_start(out=dst, in_=src),
                     waits=wg_waits, inc="wqkv", inc_n=16)
            P.op("gpsimd", lambda e, g=g: e.dma_start(
                out=woBs[g % 2][:], in_=wo_h[g * 128:(g + 1) * 128, :]),
                 waits=wg_waits, inc="wqkv", inc_n=16)
            wqkv_after[g] = P.c("wqkv")

        dma_attn_w(0)
        dma_attn_w(1)
        for g in range(4):
            wB = wBs[g % 2]
            woB = woBs[g % 2]
            WQKV = wqkv_after[g]
            for nm, bias, scale in (("q", bqf, 0.125), ("k", bkf, 1.0)):
                dstT = qT if nm == "q" else kT
                for et in range(2):
                    for ch in range(NCH):
                        gb = Gs[(et * NCH + ch) % 2]
                        bid = G_BID[(et * NCH + ch) % 2]
                        w = [("wqkv", WQKV), ("zev", zev2 - NT + (ch + 1) * 4)]
                        use_banks(w, bid)
                        def qmm(e, nm=nm, et=et, ch=ch, gb=gb, wB=wB):
                            last = None
                            for kt in range(KT):
                                last = e.matmul(
                                    gb[:],
                                    wB[nm][:, kt * 256 + et * 128: kt * 256 + (et + 1) * 128],
                                    zT[:, kt * T + ch * 512: kt * T + (ch + 1) * 512],
                                    start=(kt == 0), stop=(kt == KT - 1))
                            return last
                        P.op("tensor", qmm, waits=w, inc="qmm")
                        pc = P.op("scalar", lambda e, dstT=dstT, et=et, ch=ch, gb=gb,
                                  bias=bias, scale=scale, g=g: e.activation(
                                      dstT[:, et * T + ch * 512: et * T + (ch + 1) * 512],
                                      gb[:], AF.Identity, bias=bias[:, 2 * g + et: 2 * g + et + 1],
                                      scale=scale),
                                  waits=[("qmm", P.c("qmm")), ("ld", 96)], inc="qkev")
                        bank_hist[bid] = ("qkev", pc)
            for tt in range(NT):
                gb = Gs[tt % 2]
                bid = G_BID[tt % 2]
                w = [("wqkv", WQKV), ("zev", zev2 - NT + tt + 1)]
                use_banks(w, bid)
                def vmm(e, tt=tt, gb=gb, wB=wB):
                    last = None
                    for kt in range(KT):
                        last = e.matmul(
                            gb[:, 0:256],
                            zT[:, kt * T + tt * 128: kt * T + (tt + 1) * 128],
                            wB["v"][:, kt * 256:(kt + 1) * 256],
                            start=(kt == 0), stop=(kt == KT - 1))
                    return last
                P.op("tensor", vmm, waits=w, inc="qmm")
                pc = P.op("vector", lambda e, gb=gb, tt=tt, g=g: e.tensor_tensor(
                    vS[:, tt * 256:(tt + 1) * 256], gb[:, 0:256],
                    bvr[:, 256 * g:256 * (g + 1)], ALU.add),
                     waits=[("qmm", P.c("qmm")), ("ld", 96)], inc="vev")
                bank_hist[bid] = ("vev", pc)
            QKV_DONE = P.c("qmm")
            qmm_after[g] = QKV_DONE
            pend_fin = []
            oev_by = {}
            rslot_ctr = [0]
            for ch in (3, 2, 1, 0):
                nk = 4 * ch + 4
                for pr in range(2):
                    et = pr
                    smm_rec = {}
                    eexp_rec = {}
                    emask_rec = {}
                    pv_rec = {}
                    sd_rec = {}
                    O_i = (ch * 2 + pr) % 2
                    O2 = O2s[O_i]
                    O_bid = O2_BID[O_i]

                    def emit_S(kt):
                        tens = SS[kt % 2]
                        bids = SS_BID[kt % 2]
                        w = [("qkev", P.c("qkev"))]
                        use_banks(w, bids[0], bids[1])
                        def smm0(e, kt=kt, tens=tens, et=et, ch=ch):
                            return e.matmul(
                                tens[:, 0:512],
                                kT[0:64, et * T + kt * 128: et * T + (kt + 1) * 128],
                                qT[0:64, et * T + ch * 512: et * T + (ch + 1) * 512],
                                start=True, stop=True)
                        s0_pc = P.op("tensor", smm0, waits=w, inc="smm")
                        def smm1(e, kt=kt, tens=tens, et=et, ch=ch):
                            return e.matmul(
                                tens[:, 512:1024],
                                kT[64:128, et * T + kt * 128: et * T + (kt + 1) * 128],
                                qT[64:128, et * T + ch * 512: et * T + (ch + 1) * 512],
                                start=True, stop=True)
                        smm_rec[kt] = P.op("tensor", smm1, waits=[], inc="smm")
                        slot = kt % 3
                        pcs = []
                        for hh, spc in ((0, s0_pc), (1, smm_rec[kt])):
                            ew = [("smm", spc)]
                            if kt >= 3:
                                ew.append(("pvmm", pv_rec[kt - 3][hh]))
                                ew.append(("sdm", sd_rec[kt - 3][hh]))
                            Eap = arena[:, AE + slot * 1024 + hh * 512:
                                        AE + slot * 1024 + (hh + 1) * 512]
                            pc = P.op("scalar", lambda e, Eap=Eap, tens=tens, hh=hh:
                                      e.activation(Eap, tens[:, hh * 512:(hh + 1) * 512],
                                                   AF.Exp, bias=consts[:, 1:2], scale=1.0),
                                      waits=ew + [("const", CONST_OK)], inc="eexp")
                            pcs.append(pc)
                            bank_hist[bids[hh]] = ("eexp", pc)
                        eexp_rec[kt] = tuple(pcs)
                        if kt >= 4 * ch:
                            off = (512 * ch - 128 * kt) + 384
                            mpcs = []
                            for hh in range(2):
                                Eap = arena[:, AE + slot * 1024 + hh * 512:
                                            AE + slot * 1024 + (hh + 1) * 512]
                                mpcs.append(P.op(
                                    "vector", lambda e, Eap=Eap, off=off: e.tensor_tensor(
                                        Eap, Eap, maskw[:, off:off + 512], ALU.mult),
                                    waits=[("eexp", eexp_rec[kt][hh]), ("ld", 96)],
                                    inc="emask"))
                            emask_rec[kt] = tuple(mpcs)

                    for kt in range(min(3, nk)):
                        emit_S(kt)
                    for kt in range(nk):
                        if kt == 3:
                            while pend_fin:
                                pend_fin.pop(0)()
                        slot = kt % 3
                        Eh0 = arena[:, AE + slot * 1024: AE + slot * 1024 + 512]
                        Eh1 = arena[:, AE + slot * 1024 + 512: AE + (slot + 1) * 1024]
                        w = [("emask", emask_rec[kt][0])] if kt in emask_rec else \
                            [("eexp", eexp_rec[kt][0])]
                        w1 = [("emask", emask_rec[kt][1])] if kt in emask_rec else \
                            [("eexp", eexp_rec[kt][1])]
                        if kt == 0:
                            use_banks(w, O_bid)
                        def pv0(e, kt=kt, Eh0=Eh0, pr=pr, nk=nk, O2=O2):
                            return e.matmul(
                                O2[0:64, :], vS[:, kt * 256 + pr * 128: kt * 256 + pr * 128 + 64],
                                Eh0, start=(kt == 0), stop=(kt == nk - 1))
                        pv0_pc = P.op("tensor", pv0, waits=w, inc="pvmm")
                        def pv1(e, kt=kt, Eh1=Eh1, pr=pr, nk=nk, O2=O2):
                            return e.matmul(
                                O2[64:128, :],
                                vS[:, kt * 256 + pr * 128 + 64: kt * 256 + pr * 128 + 128],
                                Eh1, start=(kt == 0), stop=(kt == nk - 1),
                                tile_position=(0, 64))
                        pv_rec[kt] = (pv0_pc, P.op("tensor", pv1, waits=w1, inc="pvmm"))
                        wd_ = []
                        if kt == 0:
                            use_banks(wd_, "D0b")
                        def sd0(e, kt=kt, Eh0=Eh0, nk=nk):
                            return e.matmul(
                                dps[0:64, :], onesA[:, 0:64], Eh0,
                                start=(kt == 0), stop=(kt == nk - 1))
                        sd0_pc = P.op("tensor", sd0, waits=wd_, inc="sdm")
                        def sd1(e, kt=kt, Eh1=Eh1, nk=nk):
                            return e.matmul(
                                dps[64:128, :], onesA[:, 0:64], Eh1,
                                start=(kt == 0), stop=(kt == nk - 1),
                                tile_position=(0, 64))
                        sd_rec[kt] = (sd0_pc, P.op("tensor", sd1, waits=[], inc="sdm"))
                        if kt + 3 < nk:
                            emit_S(kt + 3)
                    rslot = rslot_ctr[0] % 3
                    rslot_ctr[0] += 1
                    rdh0 = rdB[0:1, rslot * 512:(rslot + 1) * 512]
                    def rd_emit(e, rslot=rslot):
                        with nc.allow_low_precision(reason="1/d as bf16 matmul operand"):
                            return e.reciprocal(
                                rdB[0:65, rslot * 512:(rslot + 1) * 512], dps[0:65, :])
                    pcr = P.op("vector", rd_emit, waits=[("sdm", sd_rec[nk - 1][1])],
                               inc="vrd")
                    bank_hist["D0b"] = ("vrd", pcr)

                    def fin(O2=O2, O_bid=O_bid, rdh0=rdh0, pr=pr, ch=ch,
                            rslot=rslot, vrd_c=P.c("vrd")):
                        w = [("vrd", vrd_c)]
                        use_banks(w, "D1a")
                        P.op("tensor", lambda e, rslot=rslot: e.matmul(
                            DR2[:, :], sel2[0:65, :],
                            rdB[0:65, rslot * 512:(rslot + 1) * 512],
                            start=True, stop=True),
                             waits=w + [("ld", 112)], inc="drmm")
                        pcd = P.op("scalar", lambda e: e.activation(
                            drB[:, :], DR2[:, :], AF.Identity),
                             waits=[("drmm", P.c("drmm"))], inc="drev")
                        bank_hist["D1a"] = ("drev", pcd)
                        pc = P.op("vector", lambda e, O2=O2, pr=pr, ch=ch:
                                  e.tensor_tensor(
                                      aT[:, pr * T + ch * 512: pr * T + (ch + 1) * 512],
                                      O2[:, :], drB[:, :], ALU.mult),
                                  waits=[("drev", P.c("drev"))], inc="oev")
                        oev_by[(ch, pr)] = pc
                        bank_hist[O_bid] = ("oev", pc)
                    pend_fin.append(fin)
            while pend_fin:
                pend_fin.pop(0)()
            for tt in [c * 4 + t_ for c in (3, 2, 1, 0) for t_ in range(4)]:
                chp = tt // 4
                OEV = max(oev_by[(chp, 0)], oev_by[(chp, 1)])
                for nh in range(2):
                    pslot = (tt * 2 + nh) % 4
                    gb = [Gs[0], Gs[1], D0[:, 0:512], D1[:, 512:1024]][pslot]
                    bid = ["A0", "A1", "D0a", "D1b"][pslot]
                    w = [("oev", OEV), ("wqkv", WQKV)]
                    use_banks(w, bid)
                    def pjm(e, tt=tt, nh=nh, gb=gb, woB=woB):
                        last = None
                        for et in range(2):
                            last = e.matmul(
                                gb[:], aT[:, et * T + tt * 128: et * T + (tt + 1) * 128],
                                woB[:, et * 1024 + nh * 512: et * 1024 + (nh + 1) * 512],
                                start=(et == 0), stop=(et == 1))
                        return last
                    P.op("tensor", pjm, waits=w, inc="pjmm")
                    xsl = x_res[:, tt * D + nh * 512: tt * D + (nh + 1) * 512]
                    if g == 0:
                        P.op("vector", lambda e, xsl=xsl, nh=nh: e.tensor_tensor(
                            xsl, xsl, bor[:, nh * 512:(nh + 1) * 512], ALU.add),
                             waits=[("ld", 96)])
                    pc = P.op("vector", lambda e, xsl=xsl, gb=gb: e.tensor_tensor(
                        xsl, xsl, gb[:], ALU.add),
                         waits=[("pjmm", P.c("pjmm"))], inc="pjev")
                    bank_hist[bid] = ("pjev", pc)
                    if g == 3 and nh == 1 and post_pjev is not None:
                        post_pjev(tt)
            pjmm_after[g] = P.c("pjmm")
            if g + 2 < 4:
                dma_attn_w(g + 2)
        return P.c("pjev")

    nexp = 1 if stage == 1 else 9
    pre2 = []
    def hook2(tt):
        ln_stats(tt, [("yev", P.c("yev"))])
        if tt % 8 == 7:
            pre2.append(ln_chain(tt // 8))
    ffn_app([], [], nexp, post_yev=hook2 if stage >= 3 else None)
    if stage >= 3:
        FFN1_ARENA_FREE = [("gmm", P.c("gmm")), ("dmm", P.c("dmm")), ("amul", P.c("amul"))]
        pre3 = [None, None]
        seen3 = [0, 0]
        def hook3(tt):
            ln_stats(tt, [("pjev", P.c("pjev"))])
            hb = tt // 8
            seen3[hb] += 1
            if seen3[hb] == 8:
                pre3[hb] = ln_chain(hb)
        attention(xgate=None, wgate=FFN1_ARENA_FREE,
                  zgate=[("gmm", P.c("gmm")), ("rmm", P.c("rmm"))], pre=pre2,
                  post_pjev=hook3 if stage >= 4 else None)
    if stage >= 4:
        ATTN_ARENA_FREE = [("pjmm", P.c("pjmm")), ("pvmm", P.c("pvmm")), ("smm", P.c("smm")),
                           ("drmm", P.c("drmm"))]
        ffn_app(xgate=None, wgate=ATTN_ARENA_FREE, nexperts=9,
                zgate=[("qmm", P.c("qmm"))], pre=pre3)

    if debug:
        P.op("sync", lambda e: e.dma_start(out=qt_d, in_=arena[:, AQ:AQ + 4096]),
             waits=[("pjev", P.c("pjev"))], inc="st", inc_n=16)
        P.op("sync", lambda e: e.dma_start(out=kt_d, in_=arena[:, AK:AK + 4096]),
             inc="st", inc_n=16)
        P.op("sync", lambda e: e.dma_start(out=vs_d, in_=arena[:, AV:AV + 4096]),
             inc="st", inc_n=16)
        P.op("sync", lambda e: e.dma_start(out=at_d, in_=arena[:, AA:AA + 8192]),
             inc="st", inc_n=16)
    out3 = out_h.rearrange("(i p) c -> p i c", p=128)
    P.op("sync", lambda e: e.dma_start(out=out3, in_=x_res[:].rearrange("p (i c) -> p i c", c=D)),
         waits=[("yev", P.c("yev")), ("pjev", P.c("pjev"))], inc="st", inc_n=16)
    P.op("sync", lambda e: None, waits=[("st", P.c("st"))])

    P.realize()
    P.ctx.close()
    return nc




_NC_CACHE = {}


def _get_nc():
    if "nc" not in _NC_CACHE:
        _NC_CACHE["nc"] = build(stage=4)
    return _NC_CACHE["nc"]


def kernel(L0_attn_embd_q, L0_attn_Wq, L0_attn_bq, L0_attn_Wk, L0_attn_bk,
           L0_attn_Wv, L0_attn_bv, L0_attn_Wo, L0_attn_bo, L0_ffn_gate_weight,
           L0_ffn_shared_gate_up, L0_ffn_shared_down, L0_ffn_experts_gate_up,
           L0_ffn_experts_down):
    inp = dict(L0_attn_embd_q=L0_attn_embd_q, L0_attn_Wq=L0_attn_Wq,
               L0_attn_bq=L0_attn_bq, L0_attn_Wk=L0_attn_Wk, L0_attn_bk=L0_attn_bk,
               L0_attn_Wv=L0_attn_Wv, L0_attn_bv=L0_attn_bv, L0_attn_Wo=L0_attn_Wo,
               L0_attn_bo=L0_attn_bo, L0_ffn_gate_weight=L0_ffn_gate_weight,
               L0_ffn_shared_gate_up=L0_ffn_shared_gate_up,
               L0_ffn_shared_down=L0_ffn_shared_down,
               L0_ffn_experts_gate_up=L0_ffn_experts_gate_up,
               L0_ffn_experts_down=L0_ffn_experts_down)
    hp = host_prep(inp)
    x = np.ascontiguousarray(np.asarray(L0_attn_embd_q, np.float32))
    nc = _get_nc()
    common = {"gup": hp["gup"], "wd": hp["wd"], "gw": hp["gw"],
              "wq": hp["wq"], "wk": hp["wk"], "wv": hp["wv"], "wo": hp["wo"],
              "bqf": hp["bqf"], "bkf": hp["bkf"], "bvr": hp["bvr"], "bor": hp["bor"],
              "ident": hp["ident"], "maskw": hp["maskw"], "sel2": hp["sel2"]}
    in_maps = [dict(common, x=np.ascontiguousarray(x[b])) for b in range(_NCORES)]
    res = run_bass_kernel_spmd(nc, in_maps, core_ids=list(range(_NCORES)))
    out = np.stack([np.asarray(res.results[b]["out"]) for b in range(_NCORES)], axis=0)
    return out


# version 28
# speedup vs baseline: 1.0165x; 1.0165x over previous
import numpy as np
import ml_dtypes
import concourse.bass as bass
import concourse.mybir as mybir
from concourse.bass_utils import run_bass_kernel_spmd
from contextlib import ExitStack

_NCORES = 8

F32 = mybir.dt.float32
BF16 = mybir.dt.bfloat16
AX = mybir.AxisListType.X
AF = mybir.ActivationFunctionType
ALU = mybir.AluOpType

T = 2048
D = 1024
NT = 16
KT = 8
NCH = 4
NE = 8
I_DIM = 938
NMT = 16
NIT = 8

ENGS = ("sync", "gpsimd", "tensor", "vector", "scalar")


class Prog:
    def __init__(self, nc):
        self.nc = nc
        self.ops = []
        self.sems = {}
        self.cnt = {}
        self.ctx = ExitStack()

    def sem(self, name):
        if name not in self.sems:
            self.sems[name] = self.ctx.enter_context(self.nc.semaphore(name))
            self.cnt[name] = 0
        return self.sems[name]

    def op(self, eng, emit, waits=(), inc=None, inc_n=1):
        wl = [(self.sem(s), v) for (s, v) in waits if v > 0]
        post = None
        if inc is not None:
            self.sem(inc)
            self.cnt[inc] += inc_n
            post = self.cnt[inc]
        self.ops.append((eng, emit, wl, (self.sems[inc], inc_n) if inc else None))
        return post

    def c(self, name):
        self.sem(name)
        return self.cnt[name]

    def realize(self):
        nc = self.nc
        with nc.Block() as blk:
            for eng_name in ENGS:
                eng_ops = [o for o in self.ops if o[0] == eng_name]
                if not eng_ops:
                    continue
                def make(eng_ops):
                    def section(eng):
                        last_wait = {}
                        for (_, emit, wl, inc) in eng_ops:
                            for (sem, val) in wl:
                                if last_wait.get(id(sem), -1) >= val:
                                    continue
                                eng.wait_ge(sem, val)
                                last_wait[id(sem)] = val
                            inst = emit(eng)
                            if inc is not None:
                                assert inst is not None
                                inst.then_inc(inc[0], inc[1])
                    return section
                getattr(blk, eng_name)(make(eng_ops))


def host_prep(inp):
    f32 = np.float32
    d = {}
    gup_list = [np.asarray(inp["L0_ffn_shared_gate_up"], f32)] + \
               [np.asarray(inp["L0_ffn_experts_gate_up"][e], f32) for e in range(NE)]
    wd_list = [np.asarray(inp["L0_ffn_shared_down"], f32)] + \
              [np.asarray(inp["L0_ffn_experts_down"][e], f32) for e in range(NE)]
    gupT = np.zeros((9, D, 2048), f32)
    for j, w in enumerate(gup_list):
        g = w[:I_DIM]
        u = w[I_DIM:]
        gupT[j, :, 0:I_DIM] = g.T
        gupT[j, :, 1024:1024 + I_DIM] = u.T
    d["gup"] = gupT.reshape(9 * D, 2048)
    wdT = np.zeros((9, D, D), f32)
    for j, w in enumerate(wd_list):
        wdT[j, :I_DIM, :] = w.T
    d["wd"] = wdT.reshape(9 * D, D)
    d["gw"] = np.ascontiguousarray(np.asarray(inp["L0_ffn_gate_weight"], f32).T)
    for nm, key in (("wq", "L0_attn_Wq"), ("wk", "L0_attn_Wk"), ("wv", "L0_attn_Wv")):
        d[nm] = np.ascontiguousarray(np.asarray(inp[key], f32).T)
    woT = np.asarray(inp["L0_attn_Wo"], f32).T
    d["wo"] = np.ascontiguousarray(
        woT.reshape(4, 2, 128, D).transpose(0, 2, 1, 3).reshape(4 * 128, 2 * D))
    d["bqf"] = np.ascontiguousarray(np.asarray(inp["L0_attn_bq"], f32).reshape(8, 128).T)
    d["bkf"] = np.ascontiguousarray(np.asarray(inp["L0_attn_bk"], f32).reshape(8, 128).T)
    d["bvr"] = np.ascontiguousarray(np.broadcast_to(np.asarray(inp["L0_attn_bv"], f32), (128, D)))
    d["bor"] = np.ascontiguousarray(np.broadcast_to(np.asarray(inp["L0_attn_bo"], f32), (128, D)))
    d["ident"] = np.eye(128, dtype=ml_dtypes.bfloat16)
    mw = (np.arange(128)[:, None] <= (np.arange(896)[None, :] - 384)).astype(ml_dtypes.bfloat16)
    d["maskw"] = mw
    sel2 = np.zeros((65, 128), ml_dtypes.bfloat16)
    sel2[0, 0:64] = 1
    sel2[64, 64:128] = 1
    d["sel2"] = sel2
    return d


def build(stage=4, debug=False):
    nc = bass.Bass("TRN2", target_bir_lowering=False)
    dr = lambda n, s, dt=F32, kind="ExternalInput": nc.dram_tensor(n, s, dt, kind=kind).ap()
    x_h = dr("x", [T, D])
    gup_h = dr("gup", [9 * D, 2048])
    wd_h = dr("wd", [9 * D, D])
    gw_h = dr("gw", [D, 8])
    wq_h = dr("wq", [D, D]); wk_h = dr("wk", [D, D]); wv_h = dr("wv", [D, D])
    wo_h = dr("wo", [512, 2 * D])
    bqf_h = dr("bqf", [128, 8]); bkf_h = dr("bkf", [128, 8])
    bvr_h = dr("bvr", [128, D]); bor_h = dr("bor", [128, D])
    id_h = dr("ident", [128, 128], BF16)
    mw_h = dr("maskw", [128, 896], BF16)
    sel2_h = dr("sel2", [65, 128], BF16)
    out_h = dr("out", [T, D], F32, kind="ExternalOutput")
    if debug:
        qt_d = dr("qt_d", [128, 4096], BF16, kind="ExternalOutput")
        kt_d = dr("kt_d", [128, 4096], BF16, kind="ExternalOutput")
        vs_d = dr("vs_d", [128, 4096], BF16, kind="ExternalOutput")
        at_d = dr("at_d", [128, 8192], BF16, kind="ExternalOutput")

    P = Prog(nc)
    ctx = P.ctx
    sb = lambda n, s, dt=F32: ctx.enter_context(nc.sbuf_tensor(n, s, dt))
    psum = lambda n, s, dt=F32: ctx.enter_context(nc.psum_tensor(n, s, dt))

    x_res = sb("x_res", [128, NT * D])
    zT = sb("zT", [128, KT * T], BF16)
    z_tm = sb("z_tm", [128, 2 * D], BF16)
    sq_scr = sb("sq_scr", [128, D], BF16)
    stat = sb("stat", [128, 144])
    logit_junk = None
    rE = sb("rE", [128, NT * 8])
    rsc = sb("rsc", [128, 96])
    logit_sb = sb("logit_sb", [128, NT * 8])
    msk8 = sb("msk8", [128, 8])
    cw = sb("cw", [128, NT * 8])
    consts = sb("consts", [128, 8])
    bqf = sb("bqf_sb", [128, 8]); bkf = sb("bkf_sb", [128, 8])
    bvr = sb("bvr_sb", [128, D]); bor = sb("bor_sb", [128, D])
    ident = sb("ident_sb", [128, 128], BF16)
    maskw = sb("maskw_sb", [128, 896], BF16)
    gwT = sb("gwT_sb", [128, KT * 8], BF16)
    sel2 = sb("sel2_sb", [128, 128], BF16)

    PS_A = psum("PS_A", [128, 1024])
    PS_B = psum("PS_B", [128, 1024])
    D0 = psum("D0", [128, 1024])
    D1 = psum("D1", [128, 1024])
    Gs = [PS_A[:, 0:512], PS_A[:, 512:1024]]
    G_BID = ["A0", "A1"]
    _bcB = PS_B[:].bitcast(BF16)
    Ts = [_bcB[:, 0:1024], _bcB[:, 1024:2048]]
    T_BID = ["B0", "B1"]
    SS = [PS_A, PS_B]
    SS_BID = [("A0", "A1"), ("B0", "B1")]

    bank_hist = {}

    def use_banks(waits, *bank_ids):
        for b in bank_ids:
            if b in bank_hist:
                waits.append(bank_hist[b])

    x3 = x_h.rearrange("(i p) c -> p i c", p=128)
    xr3 = x_res[:].rearrange("p (i c) -> p i c", c=D)
    for q, eng in enumerate(("sync", "scalar", "sync", "scalar")):
        P.op(eng, lambda e, q=q: e.dma_start(
            out=xr3[:, q * 4:(q + 1) * 4, :], in_=x3[:, q * 4:(q + 1) * 4, :]),
             inc="x%d" % q, inc_n=16)
    for dst, src in ((bqf, bqf_h), (bkf, bkf_h), (bvr, bvr_h), (bor, bor_h),
                     (ident, id_h), (maskw, mw_h)):
        P.op("sync", lambda e, d=dst, s=src: e.dma_start(out=d[:], in_=s), inc="ld", inc_n=16)
    LD_ALL = P.c("ld")
    P.op("gpsimd", lambda e: e.dma_start(
        out=gwT[:].rearrange("p (kt c) -> p kt c", kt=KT),
        in_=gw_h.rearrange("(kt p) c -> p kt c", p=128)), inc="wgw", inc_n=16)
    P.op("vector", lambda e: e.memset(consts[:, 0:1], 0.5))
    P.op("vector", lambda e: e.memset(consts[:, 1:2], -8.0))
    P.op("vector", lambda e: e.memset(consts[:, 2:3], 1e-8), inc="const")
    P.op("sync", lambda e: e.dma_start(out=sel2[0:65, :], in_=sel2_h), inc="ld", inc_n=16)
    CONST_OK = P.c("const")

    def ln_stats(i, waits):
        xall = [("x0", 16), ("x1", 16), ("x2", 16), ("x3", 16)]
        xs = x_res[:, i * D:(i + 1) * D]
        P.op("vector", lambda e, xs=xs, i=i: e.reduce_sum(stat[:, i:i+1], xs, axis=AX),
             waits=list(waits) + xall, inc="vsum")
        P.op("scalar", lambda e, xs=xs, i=i: e.activation(
            sq_scr[:], xs, AF.Square, accum_out=stat[:, 16+i:17+i]),
             waits=list(waits) + xall, inc="asq")

    def ln_chain(hb):
        hs = slice(hb * 8, hb * 8 + 8)
        sm, ss, m, m2, var = (stat[:, 0:16][:, hs], stat[:, 16:32][:, hs],
                              stat[:, 32:48][:, hs], stat[:, 48:64][:, hs],
                              stat[:, 64:80][:, hs])
        r_h, mr_h = stat[:, 96:112][:, hs], stat[:, 112:128][:, hs]
        P.op("vector", lambda e, m=m, sm=sm: e.tensor_scalar(m, sm, 1.0 / D, None, ALU.mult),
             waits=[("vsum", P.c("vsum")), ("asq", P.c("asq"))], inc="vx")
        P.op("vector", lambda e, m2=m2, m=m: e.tensor_tensor(m2, m, m, ALU.mult),
             waits=[("vx", P.c("vx"))], inc="vx")
        P.op("vector", lambda e, var=var, ss=ss, m2=m2: e.scalar_tensor_tensor(
            var, ss, 1.0 / D, m2, ALU.mult, ALU.subtract),
             waits=[("vx", P.c("vx"))], inc="vx")
        P.op("vector", lambda e, var=var: e.tensor_scalar(var, var, 1e-5, None, ALU.add),
             waits=[("vx", P.c("vx"))], inc="vx")
        P.op("vector", lambda e, var=var: e.reciprocal(var, var),
             waits=[("vx", P.c("vx"))], inc="vrv")
        P.op("scalar", lambda e, r_h=r_h, var=var: e.activation(r_h, var, AF.Sqrt),
             waits=[("vrv", P.c("vrv"))], inc="ar")
        P.op("vector", lambda e, mr_h=mr_h, m=m, r_h=r_h: e.scalar_tensor_tensor(
            mr_h, m, -1.0, r_h, ALU.mult, ALU.mult),
             waits=[("ar", P.c("ar"))], inc="vmr")
        return P.c("vmr")

    def ln_phase(xgate, zgate=(), pre=None):
        if pre is None:
            xg = xgate if callable(xgate) else (lambda i: list(xgate))
            vmr_half = []
            for hb in range(2):
                for i in range(hb * 8, hb * 8 + 8):
                    ln_stats(i, xg(i))
                vmr_half.append(ln_chain(hb))
        else:
            vmr_half = pre
        pzt_by = {}
        for i in range(NT):
            xs = x_res[:, i * D:(i + 1) * D]
            zbuf = z_tm[:, (i % 2) * D:(i % 2 + 1) * D]
            ztw = [("vmr", vmr_half[i // 8])]
            if i >= 2:
                ztw.append(("pzt", pzt_by[i - 2]))
            P.op("scalar", lambda e, xs=xs, zbuf=zbuf, i=i: e.activation(
                zbuf, xs, AF.Identity, bias=stat[:, 112+i:113+i], scale=stat[:, 96+i:97+i]),
                 waits=ztw, inc="az")
            tb = Ts[i % 2]
            bid = T_BID[i % 2]
            w = [("az", P.c("az")), ("ld", 80)]
            use_banks(w, bid)
            def tr_ops(e, zbuf=zbuf, tb=tb):
                last = None
                for kt in range(KT):
                    last = e.transpose(tb[:, kt * 128:(kt + 1) * 128],
                                       zbuf[:, kt * 128:(kt + 1) * 128], ident[:])
                return last
            P.op("tensor", tr_ops, waits=w, inc="pzt")
            pzt_by[i] = P.c("pzt")
            zdst = zT[:].rearrange("p (kt t) -> p kt t", kt=KT)[:, :, i * 128:(i + 1) * 128]
            zsrc = tb[:].rearrange("p (kt t) -> p kt t", kt=KT)
            zw = [("pzt", P.c("pzt"))] + (list(zgate) if i == 0 else [])
            pc = P.op("vector", lambda e, zdst=zdst, zsrc=zsrc: e.tensor_copy(zdst, zsrc),
                      waits=zw, inc="zev")
            bank_hist[bid] = ("zev", pc)
        return P.c("zev")

    def router_phase(zev_base):
        for i in range(NT):
            R = Gs[i % 2][:, 0:8]
            bid = G_BID[i % 2]
            w = [("zev", zev_base - NT + i + 1), ("wgw", 16)]
            use_banks(w, bid)
            def rmm(e, i=i, R=R):
                last = None
                for kt in range(KT):
                    last = e.matmul(R, zT[:, kt * T + i * 128: kt * T + (i + 1) * 128],
                                    gwT[:, kt * 8:(kt + 1) * 8],
                                    start=(kt == 0), stop=(kt == KT - 1))
                return last
            P.op("tensor", rmm, waits=w, inc="rmm")
            pc = P.op("vector", lambda e, R=R, i=i: e.tensor_copy(
                logit_sb[:, i * 8:(i + 1) * 8], R),
                 waits=[("rmm", P.c("rmm"))], inc="rlc")
            bank_hist[bid] = ("rlc", pc)
        l3 = logit_sb[:].rearrange("p (a b) -> p a b", b=8)
        rE3 = rE[:].rearrange("p (a b) -> p a b", b=8)
        cw3 = cw[:].rearrange("p (a b) -> p a b", b=8)
        Zs, m1, m2, den, rcp = (rsc[:, 0:16], rsc[:, 16:32], rsc[:, 32:48],
                                rsc[:, 48:64], rsc[:, 64:80])
        nmx = rsc[:, 80:96]
        P.op("vector", lambda e: e.reduce_max(
            nmx.rearrange("p (a b) -> p a b", b=1), l3, axis=AX, negate=True),
             waits=[("rlc", P.c("rlc"))], inc="vnm")
        VNM = P.c("vnm")
        for i in range(NT):
            P.op("scalar", lambda e, i=i: e.activation(
                rE[:, i * 8:(i + 1) * 8], logit_sb[:, i * 8:(i + 1) * 8], AF.Exp,
                bias=rsc[:, 80 + i:81 + i], scale=1.0, accum_out=rsc[:, i:i+1]),
                 waits=[("vnm", VNM), ("const", CONST_OK)], inc="rexp")
        REXP = P.c("rexp")
        m1_b = m1.unsqueeze(2).broadcast_to((128, NT, 8))
        m2_b = m2.unsqueeze(2).broadcast_to((128, NT, 8))
        rcp_b = rcp.unsqueeze(2).broadcast_to((128, NT, 8))
        P.op("vector", lambda e: e.reduce_max(
            m1.rearrange("p (a b) -> p a b", b=1), rE3, axis=AX),
             waits=[("rexp", REXP)], inc="vx")
        P.op("vector", lambda e: e.tensor_tensor(l3, rE3, m1_b, ALU.is_ge),
             waits=[("vx", P.c("vx"))], inc="vx")
        P.op("vector", lambda e: e.tensor_scalar(
            logit_sb[:], logit_sb[:], 1e30, None, ALU.mult))
        P.op("vector", lambda e: e.tensor_tensor(
            logit_sb[:], rE[:], logit_sb[:], ALU.subtract), inc="vx")
        P.op("vector", lambda e: e.reduce_max(
            m2.rearrange("p (a b) -> p a b", b=1), l3, axis=AX),
             waits=[("vx", P.c("vx"))], inc="vx")
        P.op("vector", lambda e: e.scalar_tensor_tensor(den, Zs, 1e-8, m1,
                                                        ALU.mult, ALU.add),
             waits=[("vx", P.c("vx"))], inc="vx")
        P.op("vector", lambda e: e.tensor_tensor(den, den, m2, ALU.add),
             waits=[("vx", P.c("vx"))], inc="vx")
        P.op("vector", lambda e: e.reciprocal(rcp, den),
             waits=[("vx", P.c("vx"))], inc="vx")
        P.op("vector", lambda e: e.tensor_scalar(rcp, rcp, 0.5, None, ALU.mult),
             waits=[("vx", P.c("vx"))], inc="vx")
        P.op("vector", lambda e: e.tensor_tensor(l3, rE3, m2_b, ALU.is_ge),
             waits=[("vx", P.c("vx"))], inc="vx")
        P.op("vector", lambda e: e.tensor_tensor(
            logit_sb[:], logit_sb[:], rE[:], ALU.mult))
        P.op("vector", lambda e: e.tensor_tensor(
            cw3, l3, rcp_b, ALU.mult),
             waits=[("vx", P.c("vx"))], inc="cw")
        return P.c("cw")

    arena = sb("arena", [128, 46336], BF16)
    def alloc_ffn():
        s = {}
        s["gupA"] = arena[:, 0:16384]
        s["gupB"] = arena[:, 16384:32768]
        s["wdT"] = arena[:, 32768:40960]
        s["sg"] = arena[:, 40960:45056]
        return s

    def dma_gup(s, j, waits):
        src = gup_h[j * D:(j + 1) * D, :].rearrange("(kt p) f -> p kt f", p=128)
        dst = s["gupA" if j % 2 == 0 else "gupB"][:].rearrange("p (kt f) -> p kt f", kt=KT)
        return P.op("gpsimd", lambda e: e.dma_start(out=dst, in_=src),
                    waits=waits, inc="wg", inc_n=16)

    def dma_wd(s, j, waits):
        src = wd_h[j * D:(j + 1) * D, :].rearrange("(kt p) f -> p kt f", p=128)
        dst = s["wdT"][:].rearrange("p (kt f) -> p kt f", kt=KT)
        return P.op("gpsimd", lambda e: e.dma_start(out=dst, in_=src),
                    waits=waits, inc="wd", inc_n=16)

    yev_by_tile = {}

    def expert(s, j, zev_base, cw_done, wg_need, wd_need, post_yev=None):
        gup = s["gupA" if j % 2 == 0 else "gupB"]
        MT_ORDER = [0, 8, 1, 9, 2, 10, 3, 11, 4, 12, 5, 13, 6, 14, 7, 15]
        for ch in range(NCH):
            for pos, mt in enumerate(MT_ORDER):
                gb = Gs[pos % 2]
                bid = G_BID[pos % 2]
                w = [("wg", wg_need), ("zev", zev_base - NT + (ch + 1) * 4)]
                use_banks(w, bid)
                def gmm(e, ch=ch, mt=mt, gb=gb, gup=gup):
                    last = None
                    for kt in range(KT):
                        last = e.matmul(
                            gb[:], gup[:, kt * 2048 + mt * 128: kt * 2048 + (mt + 1) * 128],
                            zT[:, kt * T + ch * 512: kt * T + (ch + 1) * 512],
                            start=(kt == 0), stop=(kt == KT - 1))
                    return last
                P.op("tensor", gmm, waits=w, inc="gmm")
                if mt < 8:
                    pc = P.op("scalar", lambda e, gb=gb, mt=mt: e.activation(
                        s["sg"][:, mt * 512:(mt + 1) * 512], gb[:], AF.Silu),
                         waits=[("gmm", P.c("gmm"))], inc="sgv")
                    bank_hist[bid] = ("sgv", pc)
                else:
                    it = mt - 8
                    pc = P.op("vector", lambda e, gb=gb, it=it: e.tensor_tensor(
                        s["sg"][:, it * 512:(it + 1) * 512],
                        s["sg"][:, it * 512:(it + 1) * 512],
                        gb[:], ALU.mult),
                         waits=[("gmm", P.c("gmm")), ("sgv", P.c("sgv"))], inc="amul")
                    bank_hist[bid] = ("amul", pc)
            amul_base = P.c("amul") - 8
            for tl in range(4):
                tt = ch * 4 + tl
                db = [D0, D1][tt % 2]
                bid0, bid1 = ("D%da" % (tt % 2)), ("D%db" % (tt % 2))
                def dmm(e, tl=tl, db=db, k0=0, k1=KT):
                    last = None
                    for kt in range(k0, k1):
                        for nh in range(2):
                            last = e.matmul(
                                db[:, nh * 512:(nh + 1) * 512],
                                s["sg"][:, kt * 512 + tl * 128: kt * 512 + (tl + 1) * 128],
                                s["wdT"][:, kt * 1024 + nh * 512: kt * 1024 + (nh + 1) * 512],
                                start=(kt == 0), stop=(kt == KT - 1))
                    return last
                for (k0, k1) in ((0, 4), (4, 7), (7, 8)):
                    w = [("amul", amul_base + k1), ("wd", wd_need)]
                    if k0 == 0:
                        use_banks(w, bid0, bid1)
                    P.op("tensor", lambda e, tl=tl, db=db, k0=k0, k1=k1: dmm(
                        e, tl, db, k0, k1), waits=w, inc="dmm")
                scal = consts[:, 0:1] if j == 0 else cw[:, tt * 8 + (j - 1): tt * 8 + j]
                wy = [("dmm", P.c("dmm"))]
                if j > 0:
                    wy.append(("cw", cw_done))
                pc = P.op("vector", lambda e, db=db, tt=tt, scal=scal: e.scalar_tensor_tensor(
                    x_res[:, tt * D:(tt + 1) * D], db[:], scal,
                    x_res[:, tt * D:(tt + 1) * D], ALU.mult, ALU.add),
                     waits=wy, inc="yev")
                yev_by_tile[tt] = pc
                bank_hist[bid0] = ("yev", pc)
                bank_hist[bid1] = ("yev", pc)
                if post_yev is not None:
                    post_yev(tt)

    def ffn_app(xgate, wgate, nexperts, zgate=(), pre=None, post_yev=None):
        zev = ln_phase(xgate, zgate, pre=pre)
        cw_done = router_phase(zev)
        s = alloc_ffn()
        dma_gup(s, 0, waits=list(wgate))
        dma_wd(s, 0, waits=list(wgate))
        if nexperts > 1:
            dma_gup(s, 1, waits=list(wgate))
        wg_counts = {0: P.c("wg") - (16 if nexperts > 1 else 0),
                     1: P.c("wg")}
        wd_count = P.c("wd")
        gup_done_hist = {}
        for j in range(nexperts):
            expert(s, j, zev, cw_done, wg_counts[min(j, 1)] if j <= 1 else wg_counts[j],
                   wd_count, post_yev=post_yev if j == nexperts - 1 else None)
            gup_done_hist[j] = P.c("gmm")
            down_done = P.c("dmm")
            if j + 2 < nexperts:
                dma_gup(s, j + 2, waits=[("gmm", gup_done_hist[j])])
                wg_counts[j + 2] = P.c("wg")
            if j + 1 < nexperts:
                dma_wd(s, j + 1, waits=[("dmm", down_done)])
                wd_count = P.c("wd")
        return P.c("yev")

    AQ, AK, AV, AE, AA, AWQ, AWK, AWV, AWO, ARD, AON, ADR = (
        0, 4096, 8192, 12352, 15424, 23616, 27712, 31808, 35904, 44096, 45696, 45760)

    def attention(xgate, wgate, zgate, pre=None, post_pjev=None):
        zev2 = ln_phase(xgate, zgate, pre=pre)
        qT = arena[:, AQ:AQ + 4096]
        kT = arena[:, AK:AK + 4096]
        vS = arena[:, AV:AV + 16 * 256]
        aT = arena[:, AA:AA + 8192]
        rdB = arena[:, ARD:ARD + 1536]
        onesA = arena[:, AON:AON + 64]
        drB = arena[:, ADR:ADR + 512]
        woBs = [arena[:, AWO:AWO + 2048], arena[:, AWO + 2048:AWO + 4096]]
        wBs = [{"q": arena[:, AWQ:AWQ + 2048], "k": arena[:, AWK:AWK + 2048],
                "v": arena[:, AWV:AWV + 2048]},
               {"q": arena[:, AWQ + 2048:AWQ + 4096], "k": arena[:, AWK + 2048:AWK + 4096],
                "v": arena[:, AWV + 2048:AWV + 4096]}]
        O2s = [D0[:, 0:512], D1[:, 512:1024]]
        O2_BID = ["D0a", "D1b"]
        dps = D0[:, 512:1024]
        DR2 = D1[:, 0:512]
        P.op("vector", lambda e: e.memset(onesA[:], 1.0), waits=list(wgate), inc="vset")
        P.op("vector", lambda e: e.memset(rdB[0:64, :], 0.0), waits=list(wgate))
        qmm_after = {}
        pjmm_after = {}
        wqkv_after = {}

        def dma_attn_w(g):
            wg_waits = list(wgate) if g < 2 else \
                [("qmm", qmm_after[g - 2]), ("pjmm", pjmm_after[g - 2])]
            for nm, wh in (("q", wq_h), ("k", wk_h), ("v", wv_h)):
                src = wh.rearrange("(kt p) e -> p kt e", p=128)[:, :, 256 * g:256 * (g + 1)]
                dst = wBs[g % 2][nm][:].rearrange("p (kt e) -> p kt e", kt=KT)
                P.op("gpsimd", lambda e, src=src, dst=dst: e.dma_start(out=dst, in_=src),
                     waits=wg_waits, inc="wqkv", inc_n=16)
            P.op("gpsimd", lambda e, g=g: e.dma_start(
                out=woBs[g % 2][:], in_=wo_h[g * 128:(g + 1) * 128, :]),
                 waits=wg_waits, inc="wqkv", inc_n=16)
            wqkv_after[g] = P.c("wqkv")

        dma_attn_w(0)
        dma_attn_w(1)
        for g in range(4):
            wB = wBs[g % 2]
            woB = woBs[g % 2]
            WQKV = wqkv_after[g]
            for nm, bias, scale in (("q", bqf, 0.125), ("k", bkf, 1.0)):
                dstT = qT if nm == "q" else kT
                for et in range(2):
                    for ch in range(NCH):
                        gb = Gs[(et * NCH + ch) % 2]
                        bid = G_BID[(et * NCH + ch) % 2]
                        w = [("wqkv", WQKV), ("zev", zev2 - NT + (ch + 1) * 4)]
                        use_banks(w, bid)
                        def qmm(e, nm=nm, et=et, ch=ch, gb=gb, wB=wB):
                            last = None
                            for kt in range(KT):
                                last = e.matmul(
                                    gb[:],
                                    wB[nm][:, kt * 256 + et * 128: kt * 256 + (et + 1) * 128],
                                    zT[:, kt * T + ch * 512: kt * T + (ch + 1) * 512],
                                    start=(kt == 0), stop=(kt == KT - 1))
                            return last
                        P.op("tensor", qmm, waits=w, inc="qmm")
                        pc = P.op("scalar", lambda e, dstT=dstT, et=et, ch=ch, gb=gb,
                                  bias=bias, scale=scale, g=g: e.activation(
                                      dstT[:, et * T + ch * 512: et * T + (ch + 1) * 512],
                                      gb[:], AF.Identity, bias=bias[:, 2 * g + et: 2 * g + et + 1],
                                      scale=scale),
                                  waits=[("qmm", P.c("qmm")), ("ld", 96)], inc="qkev")
                        bank_hist[bid] = ("qkev", pc)
            for tt in range(NT):
                gb = Gs[tt % 2]
                bid = G_BID[tt % 2]
                w = [("wqkv", WQKV), ("zev", zev2 - NT + tt + 1)]
                use_banks(w, bid)
                def vmm(e, tt=tt, gb=gb, wB=wB):
                    last = None
                    for kt in range(KT):
                        last = e.matmul(
                            gb[:, 0:256],
                            zT[:, kt * T + tt * 128: kt * T + (tt + 1) * 128],
                            wB["v"][:, kt * 256:(kt + 1) * 256],
                            start=(kt == 0), stop=(kt == KT - 1))
                    return last
                P.op("tensor", vmm, waits=w, inc="qmm")
                pc = P.op("vector", lambda e, gb=gb, tt=tt, g=g: e.tensor_tensor(
                    vS[:, tt * 256:(tt + 1) * 256], gb[:, 0:256],
                    bvr[:, 256 * g:256 * (g + 1)], ALU.add),
                     waits=[("qmm", P.c("qmm")), ("ld", 96)], inc="vev")
                bank_hist[bid] = ("vev", pc)
            QKV_DONE = P.c("qmm")
            qmm_after[g] = QKV_DONE
            pend_fin = []
            oev_by = {}
            rslot_ctr = [0]
            for ch in (3, 2, 1, 0):
                nk = 4 * ch + 4
                for pr in range(2):
                    et = pr
                    smm_rec = {}
                    eexp_rec = {}
                    emask_rec = {}
                    pv_rec = {}
                    sd_rec = {}
                    O_i = (ch * 2 + pr) % 2
                    O2 = O2s[O_i]
                    O_bid = O2_BID[O_i]

                    def emit_S(kt):
                        SL5 = [PS_A[:, 0:512], PS_A[:, 512:1024],
                               PS_B[:, 0:512], PS_B[:, 512:1024], D1[:, 0:512]]
                        BI5 = ["A0", "A1", "B0", "B1", "D1a"]
                        s0 = (2 * kt) % 5
                        s1 = (2 * kt + 1) % 5
                        w = [("qkev", P.c("qkev"))]
                        use_banks(w, BI5[s0])
                        def smm0(e, kt=kt, s0=s0, et=et, ch=ch):
                            return e.matmul(
                                SL5[s0],
                                kT[0:64, et * T + kt * 128: et * T + (kt + 1) * 128],
                                qT[0:64, et * T + ch * 512: et * T + (ch + 1) * 512],
                                start=True, stop=True)
                        s0_pc = P.op("tensor", smm0, waits=w, inc="smm")
                        w1 = []
                        use_banks(w1, BI5[s1])
                        def smm1(e, kt=kt, s1=s1, et=et, ch=ch):
                            return e.matmul(
                                SL5[s1],
                                kT[64:128, et * T + kt * 128: et * T + (kt + 1) * 128],
                                qT[64:128, et * T + ch * 512: et * T + (ch + 1) * 512],
                                start=True, stop=True)
                        smm_rec[kt] = P.op("tensor", smm1, waits=w1, inc="smm")
                        slot = kt % 3
                        pcs = []
                        for hh, spc, sb_ in ((0, s0_pc, s0), (1, smm_rec[kt], s1)):
                            ew = [("smm", spc)]
                            if kt >= 3:
                                ew.append(("pvmm", pv_rec[kt - 3][hh]))
                                ew.append(("sdm", sd_rec[kt - 3][hh]))
                            Eap = arena[:, AE + slot * 1024 + hh * 512:
                                        AE + slot * 1024 + (hh + 1) * 512]
                            pc = P.op("scalar", lambda e, Eap=Eap, sb_=sb_:
                                      e.activation(Eap, SL5[sb_],
                                                   AF.Exp, bias=consts[:, 1:2], scale=1.0),
                                      waits=ew + [("const", CONST_OK)], inc="eexp")
                            pcs.append(pc)
                            bank_hist[BI5[sb_]] = ("eexp", pc)
                        eexp_rec[kt] = tuple(pcs)
                        if kt >= 4 * ch:
                            off = (512 * ch - 128 * kt) + 384
                            mpcs = []
                            for hh in range(2):
                                Eap = arena[:, AE + slot * 1024 + hh * 512:
                                            AE + slot * 1024 + (hh + 1) * 512]
                                mpcs.append(P.op(
                                    "vector", lambda e, Eap=Eap, off=off: e.tensor_tensor(
                                        Eap, Eap, maskw[:, off:off + 512], ALU.mult),
                                    waits=[("eexp", eexp_rec[kt][hh]), ("ld", 96)],
                                    inc="emask"))
                            emask_rec[kt] = tuple(mpcs)

                    for kt in range(min(3, nk)):
                        emit_S(kt)
                    for kt in range(nk):
                        if kt == 3:
                            while pend_fin:
                                pend_fin.pop(0)()
                        slot = kt % 3
                        Eh0 = arena[:, AE + slot * 1024: AE + slot * 1024 + 512]
                        Eh1 = arena[:, AE + slot * 1024 + 512: AE + (slot + 1) * 1024]
                        w = [("emask", emask_rec[kt][0])] if kt in emask_rec else \
                            [("eexp", eexp_rec[kt][0])]
                        w1 = [("emask", emask_rec[kt][1])] if kt in emask_rec else \
                            [("eexp", eexp_rec[kt][1])]
                        if kt == 0:
                            use_banks(w, O_bid)
                        def pv0(e, kt=kt, Eh0=Eh0, pr=pr, nk=nk, O2=O2):
                            return e.matmul(
                                O2[0:64, :], vS[:, kt * 256 + pr * 128: kt * 256 + pr * 128 + 64],
                                Eh0, start=(kt == 0), stop=(kt == nk - 1))
                        pv0_pc = P.op("tensor", pv0, waits=w, inc="pvmm")
                        def pv1(e, kt=kt, Eh1=Eh1, pr=pr, nk=nk, O2=O2):
                            return e.matmul(
                                O2[64:128, :],
                                vS[:, kt * 256 + pr * 128 + 64: kt * 256 + pr * 128 + 128],
                                Eh1, start=(kt == 0), stop=(kt == nk - 1),
                                tile_position=(0, 64))
                        pv_rec[kt] = (pv0_pc, P.op("tensor", pv1, waits=w1, inc="pvmm"))
                        wd_ = []
                        if kt == 0:
                            use_banks(wd_, "D0b")
                        def sd0(e, kt=kt, Eh0=Eh0, nk=nk):
                            return e.matmul(
                                dps[0:64, :], onesA[:, 0:64], Eh0,
                                start=(kt == 0), stop=(kt == nk - 1))
                        sd0_pc = P.op("tensor", sd0, waits=wd_, inc="sdm")
                        def sd1(e, kt=kt, Eh1=Eh1, nk=nk):
                            return e.matmul(
                                dps[64:128, :], onesA[:, 0:64], Eh1,
                                start=(kt == 0), stop=(kt == nk - 1),
                                tile_position=(0, 64))
                        sd_rec[kt] = (sd0_pc, P.op("tensor", sd1, waits=[], inc="sdm"))
                        if kt + 3 < nk:
                            emit_S(kt + 3)
                    rslot = rslot_ctr[0] % 3
                    rslot_ctr[0] += 1
                    rdh0 = rdB[0:1, rslot * 512:(rslot + 1) * 512]
                    def rd_emit(e, rslot=rslot):
                        with nc.allow_low_precision(reason="1/d as bf16 matmul operand"):
                            return e.reciprocal(
                                rdB[0:65, rslot * 512:(rslot + 1) * 512], dps[0:65, :])
                    pcr = P.op("vector", rd_emit, waits=[("sdm", sd_rec[nk - 1][1])],
                               inc="vrd")
                    bank_hist["D0b"] = ("vrd", pcr)

                    def fin(O2=O2, O_bid=O_bid, rdh0=rdh0, pr=pr, ch=ch,
                            rslot=rslot, vrd_c=P.c("vrd")):
                        w = [("vrd", vrd_c)]
                        use_banks(w, "D1a")
                        P.op("tensor", lambda e, rslot=rslot: e.matmul(
                            DR2[:, :], sel2[0:65, :],
                            rdB[0:65, rslot * 512:(rslot + 1) * 512],
                            start=True, stop=True),
                             waits=w + [("ld", 112)], inc="drmm")
                        pcd = P.op("scalar", lambda e: e.activation(
                            drB[:, :], DR2[:, :], AF.Identity),
                             waits=[("drmm", P.c("drmm"))], inc="drev")
                        bank_hist["D1a"] = ("drev", pcd)
                        pc = P.op("vector", lambda e, O2=O2, pr=pr, ch=ch:
                                  e.tensor_tensor(
                                      aT[:, pr * T + ch * 512: pr * T + (ch + 1) * 512],
                                      O2[:, :], drB[:, :], ALU.mult),
                                  waits=[("drev", P.c("drev"))], inc="oev")
                        oev_by[(ch, pr)] = pc
                        bank_hist[O_bid] = ("oev", pc)
                    pend_fin.append(fin)
            while pend_fin:
                pend_fin.pop(0)()
            for tt in [c * 4 + t_ for c in (3, 2, 1, 0) for t_ in range(4)]:
                chp = tt // 4
                OEV = max(oev_by[(chp, 0)], oev_by[(chp, 1)])
                for nh in range(2):
                    gb = Gs[(tt * 2 + nh) % 2]
                    bid = G_BID[(tt * 2 + nh) % 2]
                    w = [("oev", OEV), ("wqkv", WQKV)]
                    use_banks(w, bid)
                    def pjm(e, tt=tt, nh=nh, gb=gb, woB=woB):
                        last = None
                        for et in range(2):
                            last = e.matmul(
                                gb[:], aT[:, et * T + tt * 128: et * T + (tt + 1) * 128],
                                woB[:, et * 1024 + nh * 512: et * 1024 + (nh + 1) * 512],
                                start=(et == 0), stop=(et == 1))
                        return last
                    P.op("tensor", pjm, waits=w, inc="pjmm")
                    xsl = x_res[:, tt * D + nh * 512: tt * D + (nh + 1) * 512]
                    if g == 0:
                        P.op("vector", lambda e, xsl=xsl, nh=nh: e.tensor_tensor(
                            xsl, xsl, bor[:, nh * 512:(nh + 1) * 512], ALU.add),
                             waits=[("ld", 96)])
                    pc = P.op("vector", lambda e, xsl=xsl, gb=gb: e.tensor_tensor(
                        xsl, xsl, gb[:], ALU.add),
                         waits=[("pjmm", P.c("pjmm"))], inc="pjev")
                    bank_hist[bid] = ("pjev", pc)
                    if g == 3 and nh == 1 and post_pjev is not None:
                        post_pjev(tt)
            pjmm_after[g] = P.c("pjmm")
            if g + 2 < 4:
                dma_attn_w(g + 2)
        return P.c("pjev")

    nexp = 1 if stage == 1 else 9
    pre2 = []
    def hook2(tt):
        ln_stats(tt, [("yev", P.c("yev"))])
        if tt % 8 == 7:
            pre2.append(ln_chain(tt // 8))
    ffn_app([], [], nexp, post_yev=hook2 if stage >= 3 else None)
    if stage >= 3:
        FFN1_ARENA_FREE = [("gmm", P.c("gmm")), ("dmm", P.c("dmm")), ("amul", P.c("amul"))]
        pre3 = [None, None]
        seen3 = [0, 0]
        def hook3(tt):
            ln_stats(tt, [("pjev", P.c("pjev"))])
            hb = tt // 8
            seen3[hb] += 1
            if seen3[hb] == 8:
                pre3[hb] = ln_chain(hb)
        attention(xgate=None, wgate=FFN1_ARENA_FREE,
                  zgate=[("gmm", P.c("gmm")), ("rmm", P.c("rmm"))], pre=pre2,
                  post_pjev=hook3 if stage >= 4 else None)
    if stage >= 4:
        ATTN_ARENA_FREE = [("pjmm", P.c("pjmm")), ("pvmm", P.c("pvmm")), ("smm", P.c("smm")),
                           ("drmm", P.c("drmm"))]
        ffn_app(xgate=None, wgate=ATTN_ARENA_FREE, nexperts=9,
                zgate=[("qmm", P.c("qmm"))], pre=pre3)

    if debug:
        P.op("sync", lambda e: e.dma_start(out=qt_d, in_=arena[:, AQ:AQ + 4096]),
             waits=[("pjev", P.c("pjev"))], inc="st", inc_n=16)
        P.op("sync", lambda e: e.dma_start(out=kt_d, in_=arena[:, AK:AK + 4096]),
             inc="st", inc_n=16)
        P.op("sync", lambda e: e.dma_start(out=vs_d, in_=arena[:, AV:AV + 4096]),
             inc="st", inc_n=16)
        P.op("sync", lambda e: e.dma_start(out=at_d, in_=arena[:, AA:AA + 8192]),
             inc="st", inc_n=16)
    out3 = out_h.rearrange("(i p) c -> p i c", p=128)
    P.op("sync", lambda e: e.dma_start(out=out3, in_=x_res[:].rearrange("p (i c) -> p i c", c=D)),
         waits=[("yev", P.c("yev")), ("pjev", P.c("pjev"))], inc="st", inc_n=16)
    P.op("sync", lambda e: None, waits=[("st", P.c("st"))])

    P.realize()
    P.ctx.close()
    return nc




_NC_CACHE = {}


def _get_nc():
    if "nc" not in _NC_CACHE:
        _NC_CACHE["nc"] = build(stage=4)
    return _NC_CACHE["nc"]


def kernel(L0_attn_embd_q, L0_attn_Wq, L0_attn_bq, L0_attn_Wk, L0_attn_bk,
           L0_attn_Wv, L0_attn_bv, L0_attn_Wo, L0_attn_bo, L0_ffn_gate_weight,
           L0_ffn_shared_gate_up, L0_ffn_shared_down, L0_ffn_experts_gate_up,
           L0_ffn_experts_down):
    inp = dict(L0_attn_embd_q=L0_attn_embd_q, L0_attn_Wq=L0_attn_Wq,
               L0_attn_bq=L0_attn_bq, L0_attn_Wk=L0_attn_Wk, L0_attn_bk=L0_attn_bk,
               L0_attn_Wv=L0_attn_Wv, L0_attn_bv=L0_attn_bv, L0_attn_Wo=L0_attn_Wo,
               L0_attn_bo=L0_attn_bo, L0_ffn_gate_weight=L0_ffn_gate_weight,
               L0_ffn_shared_gate_up=L0_ffn_shared_gate_up,
               L0_ffn_shared_down=L0_ffn_shared_down,
               L0_ffn_experts_gate_up=L0_ffn_experts_gate_up,
               L0_ffn_experts_down=L0_ffn_experts_down)
    hp = host_prep(inp)
    x = np.ascontiguousarray(np.asarray(L0_attn_embd_q, np.float32))
    nc = _get_nc()
    common = {"gup": hp["gup"], "wd": hp["wd"], "gw": hp["gw"],
              "wq": hp["wq"], "wk": hp["wk"], "wv": hp["wv"], "wo": hp["wo"],
              "bqf": hp["bqf"], "bkf": hp["bkf"], "bvr": hp["bvr"], "bor": hp["bor"],
              "ident": hp["ident"], "maskw": hp["maskw"], "sel2": hp["sel2"]}
    in_maps = [dict(common, x=np.ascontiguousarray(x[b])) for b in range(_NCORES)]
    res = run_bass_kernel_spmd(nc, in_maps, core_ids=list(range(_NCORES)))
    out = np.stack([np.asarray(res.results[b]["out"]) for b in range(_NCORES)], axis=0)
    return out
